# version 04 (final)
# baseline (speedup 1.0000x reference)
import os
import numpy as np
import concourse.bass as bass
import concourse.mybir as mybir
from concourse.bass_utils import run_bass_kernel_spmd
from contextlib import ExitStack

F32 = mybir.dt.float32
BF16 = mybir.dt.bfloat16
AF = mybir.ActivationFunctionType
ALU = mybir.AluOpType
AX = mybir.AxisListType

N_CORES = 8
B_CORE = 4
L = 2048
DM = 1024
DI = 2048
NST = 16
EXPERTS = 8
II = 512
IS = 2048
T_C = 512
NTC = L // T_C
DIT = DI // 128
DMT = DM // 128
IST = IS // 128
TT = T_C // 128

ENGINES = ("PE", "ACT", "DVE", "GP", "SP")
SELF_SEM_ENGINES = ("ACT", "GP")
KB = 1024


class Op:
    __slots__ = ("engine", "emit", "reads", "writes", "dma_sem", "need_inc",
                 "inc_val", "deps", "slow", "sreads")

    def __init__(self, engine, emit, reads, writes, dma_sem, slow=False):
        self.engine = engine
        self.emit = emit
        self.reads = reads
        self.writes = writes
        self.dma_sem = dma_sem
        self.need_inc = dma_sem is not None
        self.inc_val = None
        self.deps = []
        self.slow = slow
        self.sreads = ()


class Rec:
    def __init__(self):
        self.max_ops = int(os.environ.get("K_MAXOPS", "999999999"))
        self.ops = []
        self.last_writer = {}
        self.readers = {}
        self.dma_counts = {}
        self.dma_rr = {}
        self.dma_last = {}

    def op(self, engine, emit, reads=(), writes=(), slow=False, sreads=()):
        o = Op(engine, emit, tuple(reads) + tuple(sreads), tuple(writes), None,
               slow=slow)
        o.sreads = tuple(sreads)
        if len(self.ops) >= self.max_ops:
            return o
        self._record(o)
        return o

    DMA_POOL = {"dW": 6, "dA": 3, "dS": 3, "dG": 2}

    def dma(self, issue_engine, family, emit, reads=(), writes=()):
        i = self.dma_rr.get(family, 0)
        self.dma_rr[family] = i + 1
        sem_name = f"{family}{i % self.DMA_POOL[family]}"
        o = Op(issue_engine, emit, tuple(reads), tuple(writes), sem_name)
        if len(self.ops) >= self.max_ops:
            return o
        self.dma_counts[sem_name] = self.dma_counts.get(sem_name, 0) + 1
        o.inc_val = 16 * self.dma_counts[sem_name]
        self._record(o)
        prev = self.dma_last.get(sem_name)
        if prev is not None and prev not in o.deps:
            o.deps.append(prev)
        self.dma_last[sem_name] = o
        return o

    def _record(self, o):
        deps = set()
        hard = set()
        for k in o.reads:
            w = self.last_writer.get(k)
            if w is not None:
                deps.add(w)
                if k in o.sreads:
                    hard.add(w)
        for k in o.writes:
            w = self.last_writer.get(k)
            if w is not None:
                deps.add(w)
            for r in self.readers.get(k, ()):
                deps.add(r)
        final = []
        for d in deps:
            if d is o:
                continue
            if d.dma_sem is None and d.engine == o.engine and \
               o.engine not in SELF_SEM_ENGINES and not d.slow and \
               d not in hard:
                continue
            if d.dma_sem is None:
                d.need_inc = True
            final.append(d)
        o.deps = final
        self.ops.append(o)
        for k in o.reads:
            self.readers.setdefault(k, []).append(o)
        for k in o.writes:
            self.last_writer[k] = o
            self.readers[k] = []

    def fence(self):
        allk = list(self.last_writer.keys())
        for e in ENGINES:
            o = Op(e, None, tuple(allk), (), None)
            if len(self.ops) >= self.max_ops:
                continue
            deps = set()
            for k in allk:
                w = self.last_writer.get(k)
                if w is not None:
                    deps.add(w)
                for r in self.readers.get(k, ()):
                    deps.add(r)
            final = []
            for d in deps:
                if d.dma_sem is None and d.engine == e:
                    continue
                if d.dma_sem is None:
                    d.need_inc = True
                final.append(d)
            o.deps = final
            self.ops.append(o)

    def finalize(self):
        counts = {e: 0 for e in ENGINES}
        for o in self.ops:
            if o.dma_sem is None and o.need_inc:
                counts[o.engine] += 1
                o.inc_val = counts[o.engine]
        return counts

    def emit_engine(self, engine, eng, sems, dma_sems):
        cur = {}
        for o in self.ops:
            if o.engine != engine:
                continue
            for d in o.deps:
                if d.dma_sem is not None:
                    sem, tgt = dma_sems[d.dma_sem], d.inc_val
                else:
                    if not d.need_inc:
                        continue
                    sem, tgt = sems[d.engine], d.inc_val
                key = id(sem)
                if cur.get(key, 0) < tgt:
                    eng.wait_ge(sem, tgt)
                    cur[key] = tgt
            if o.emit is None:
                continue
            inst = o.emit(eng)
            if o.dma_sem is not None:
                inst.then_inc(dma_sems[o.dma_sem], 16)
            elif o.need_inc:
                inst.then_inc(sems[o.engine], 1)


class Arena:
    def __init__(self, nc, name, kb_per_part):
        self.cm = nc.sbuf_tensor(name, [128, kb_per_part * KB], mybir.dt.uint8)
        self.handle = None

    def __enter__(self):
        self.handle = self.cm.__enter__()
        return self

    def __exit__(self, *a):
        return self.cm.__exit__(*a)

    def view(self, off_bytes, shape, dtype, parts=128):
        n = int(np.prod(shape[1:])) * mybir.dt.size(dtype)
        assert off_bytes % 32 == 0, off_bytes
        ap = self.handle[0:parts, off_bytes:off_bytes + n].bitcast(dtype)
        if len(shape) > 2:
            dims = " ".join(f"d{i}" for i in range(1, len(shape)))
            kw = {f"d{i}": shape[i] for i in range(1, len(shape) - 1)}
            ap = ap.rearrange(f"p ({dims}) -> p {dims}", **kw)
        return ap


def build_program():
    nc = bass.Bass("TRN2", target_bir_lowering=False, debug=False)

    def din(name, shape, dtype=F32):
        return nc.dram_tensor(name, list(shape), dtype, kind="ExternalInput").ap()

    x_in = din("x", (B_CORE, L, DM))
    wlT = din("wlT", (DM, DI))
    wdT = din("wdT", (DI, DI))
    wrT = din("wrT", (DM, DI))
    woT = din("woT", (DI, DM))
    wbc = din("wbc", (DI, 2 * NST))
    convd = din("convd", (DIT, 4, 128, 128))
    convb = din("convb", (DI,))
    bdel = din("bdel", (DI,))
    a_log = din("a_log", (NST,))
    ln_g = din("ln_g", (DI,))
    ln_b = din("ln_b", (DI,))
    rw = din("rw", (DM, EXPERTS))
    rbias = din("rbias", (1, 1))
    gw = din("gw", (EXPERTS, DM, II))
    uw = din("uw", (EXPERTS, DM, II))
    dw = din("dw", (EXPERTS, II, DM))
    sgw = din("sgw", (DM, IS))
    suw = din("suw", (DM, IS))
    sdw = din("sdw", (IS, DM))
    eye_in = din("eye", (128, 128))
    sel16_in = din("sel16i", (NST, NST * 128))
    ones1_in = din("ones1", (1, 128))

    out = nc.dram_tensor("out", [B_CORE, L, DM], F32, kind="ExternalOutput").ap()

    def scr(name, shape, dtype):
        return nc.dram_tensor(name, list(shape), dtype, kind="Internal").ap()

    wlT_b = scr("wlT_b", (DM, DI), BF16)
    wdT_b = scr("wdT_b", (DI, DI), BF16)
    wrT_b = scr("wrT_b", (DM, DI), BF16)
    woT_b = scr("woT_b", (DI, DM), BF16)
    wbc_b = scr("wbc_b", (DI, 2 * NST), BF16)
    convd_b = scr("convd_b", (DIT, 4, 128, 128), BF16)
    gw_b = scr("gw_b", (EXPERTS, DM, II), BF16)
    uw_b = scr("uw_b", (EXPERTS, DM, II), BF16)
    dw_b = scr("dw_b", (EXPERTS, II, DM), BF16)
    sgw_b = scr("sgw_b", (DM, IS), BF16)
    suw_b = scr("suw_b", (DM, IS), BF16)
    sdw_b = scr("sdw_b", (IS, DM), BF16)
    h2_sp = nc.dram_tensor("h2_sp", [B_CORE, L, DM], F32,
                           kind="ExternalOutput").ap()
    dbg_wdn = nc.dram_tensor("dbg_wdn", [128, L // 128, 8], F32,
                             kind="ExternalOutput").ap()
    dbg_wbc = nc.dram_tensor("dbg_wbc", [128, 64], F32,
                             kind="ExternalOutput").ap()

    rec = Rec()
    SP, ACT, DVE, GP, PE = "SP", "ACT", "DVE", "GP", "PE"

    with ExitStack() as ctx:
        ar1 = ctx.enter_context(Arena(nc, "ar1", 64))
        ar2 = ctx.enter_context(Arena(nc, "ar2", 32))
        ar3 = ctx.enter_context(Arena(nc, "ar3", 31))
        ar4 = ctx.enter_context(Arena(nc, "ar4", 32))
        ar5 = ctx.enter_context(Arena(nc, "ar5", 24))
        ar6 = ctx.enter_context(Arena(nc, "ar6", 8))

        def small(name, shape, dtype=F32):
            return ctx.enter_context(nc.sbuf_tensor(name, list(shape), dtype))

        a_bc = small("a_bc", (128, NST))
        ident_f = small("ident_f", (128, 128))
        ident_b = small("ident_b", (128, 128), BF16)
        ones1 = small("ones1s", (1, 128))
        onesc = small("onesc", (128, 1))
        bdel_s = small("bdel_s", (128, DIT))
        convb_s = small("convb_s", (128, DIT))
        g_s = small("g_s", (128, DIT))
        lnb_s = small("lnb_s", (128, DIT), BF16)
        rbias_s = small("rbias_s", (128, 1))
        rw_s = small("rw_s", (128, DMT, EXPERTS), BF16)
        hstate = small("hstate", (128, DIT * NST))
        tails = small("tails", (128, DIT, 3), BF16)
        arow = small("arow", (1, NST))
        arow2 = small("arow2", (1, NST))
        stats = small("stats", (128, 12))
        mvv = small("mvv", (128, 2))
        c_sq = small("c_sq", (128, 1))
        c_r = small("c_r", (128, 1))
        c_mr = small("c_mr", (128, 1))
        bvec = small("bvec", (128, DMT))
        rr = small("rr", (1, 3 * T_C))
        rmax = small("rmax", (128, 1))
        rsum = small("rsum", (128, 1))
        top8 = small("top8", (128, 8))
        maskt = small("maskt", (128, 8))
        wsum = small("wsum", (128, 1))
        scoret = small("scoret", (128, TT, 8))
        wdnt = small("wdnt", (128, TT, 8))
        sel16 = small("sel16", (NST, NST * 128), BF16)
        wdn_all = small("wdn_all", (128, L // 128, 8))
        dbg_sb = small("dbg_sb", (128, 64))

        pA = ctx.enter_context(nc.psum_tensor("pA", [128, 512], F32))
        pB = ctx.enter_context(nc.psum_tensor("pB", [128, 512], F32))
        yA = ctx.enter_context(nc.psum_tensor("yA", [128, 512], F32))
        yB = ctx.enter_context(nc.psum_tensor("yB", [128, 512], F32))
        pT = ctx.enter_context(nc.psum_tensor("pT", [128, 1024], F32))
        pR = ctx.enter_context(nc.psum_tensor("pR", [128, 512], F32))
        pX = ctx.enter_context(nc.psum_tensor("pX", [128, 128], BF16))

        sems = {e: ctx.enter_context(nc.semaphore(f"sem_{e}")) for e in ENGINES}
        dma_sems = {}
        for fam, n in Rec.DMA_POOL.items():
            for i in range(n):
                s = f"{fam}{i}"
                dma_sems[s] = ctx.enter_context(nc.semaphore(f"dma_{s}"))
        block = ctx.enter_context(nc.Block())

        Bb = ar1.view(0, [128, NST, T_C], BF16)
        Cb = ar1.view(16 * KB, [128, NST, T_C], BF16)
        z_sl = ar1.view(32 * KB, [128, DIT, T_C], F32)
        zp_b = ar1.view(0, [128, DIT, T_C], BF16)
        mo_fm = ar1.view(32 * KB, [128, DMT, T_C], F32)
        routed = ar1.view(0, [128, DMT, L], F32)

        xl32 = ar2.view(0, [128, DIT, T_C], F32)
        h2_sl = ar2.view(0, [128, TT, DM], F32)
        x_sl = ar2.view(16 * KB, [128, TT, DM], F32)
        xn2_fm = ar2.view(0, [128, DMT, L], BF16)

        xl_bf = ar3.view(0, [128, DIT, T_C], BF16)
        leftb = [ar3.view(16 * KB + i * 1056, [128, T_C + 3], BF16)
                 for i in range(2)]
        Gl = [ar3.view(20992 + i * 2 * KB, [128, T_C], F32) for i in range(2)]
        ul = [ar3.view(25088 + i * 2 * KB, [128, T_C], F32) for i in range(2)]
        ychl = [ar3.view(29184 + i * KB, [128, T_C], BF16) for i in range(2)]
        sgst = ar3.view(0, [128, DMT, 4, 128], BF16)
        sust = ar3.view(8 * KB, [128, DMT, 4, 128], BF16)
        saug = ar3.view(16 * KB, [128, 4, T_C], BF16)

        xn1_tm = ar4.view(0, [128, TT, DM], BF16)
        xn1_fm = ar4.view(8 * KB, [128, DMT, T_C], BF16)
        hl = [ar4.view(16 * KB + i * 2 * KB, [128, T_C], F32) for i in range(2)]
        sgl = [ar4.view(20 * KB + i * 2 * KB, [128, T_C], F32) for i in range(2)]
        dl = [ar4.view(24 * KB + i * 2 * KB, [128, T_C], F32) for i in range(2)]
        dxl = [ar4.view(28 * KB + i * 2 * KB, [128, T_C], F32) for i in range(2)]
        out_tile = ar4.view(16 * KB, [128, DM], F32)
        ap_sl = [ar4.view(24 * KB + i * KB, [128, T_C], BF16) for i in range(4)]

        wl_st = [ar5.view(i * 2 * KB, [128, DMT, 128], BF16) for i in range(2)]
        wr_st = [ar5.view(4 * KB + i * 2 * KB, [128, DMT, 128], BF16)
                 for i in range(2)]
        wo_st = [ar5.view(8 * KB + i * 2 * KB, [128, DM], BF16) for i in range(2)]
        wd_st = [ar5.view(12 * KB + i * 4 * KB, [128, DIT, 128], BF16)
                 for i in range(2)]
        dg_st = [ar5.view(20 * KB + i * KB, [128, 4, 128], BF16) for i in range(2)]
        h2r = (ar2.view(0, [128, TT, DM], F32)
               if os.environ.get("K_HACK2") == "h2ar2"
               else ar5.view(0, [128, TT, DM], F32))
        sd_st = ar5.view(0, [128, 4, DM], BF16)
        wg_st = ar5.view(0, [128, DMT, II], BF16)
        wu_st = ar5.view(8 * KB, [128, DMT, II], BF16)
        wdown_st = ar5.view(16 * KB, [128, 4, DM], BF16)

        wbc_s = ar6.view(0, [128, DIT, 2 * NST], BF16)
        bc_bf = ar6.view(1 * KB, [128, T_C], BF16, parts=32)
        cc_bf = ar6.view(2 * KB, [128, T_C], BF16, parts=16)
        wbcast = ar6.view(3 * KB, [128, T_C], F32)

        def ldw(dst, src, key, eng=SP, sem="dW", slow=False):
            rec.dma(eng, sem, lambda e, d=dst, s=src, sl=slow: e.dma_start(
                out=d, in_=s, allow_slow_non_contiguous=sl), writes=[key])

        ldw(ident_f[:], eye_in, "ident_f")
        ldw(ones1[:], ones1_in, "ones1")
        rec.dma(SP, "dW", lambda e: e.dma_start(
            out=onesc[:], in_=ones1_in.rearrange("a b -> b a"),
            allow_slow_non_contiguous=True), writes=["onesc"])
        ldw(bdel_s[:], bdel.rearrange("(a b) -> b a", b=128), "bdel_s", slow=True)
        ldw(convb_s[:], convb.rearrange("(a b) -> b a", b=128), "convb_s", slow=True)
        ldw(g_s[:], ln_g.rearrange("(a b) -> b a", b=128), "g_s", slow=True)
        rec.dma(GP, "dG", lambda e: e.dma_start(
            out=lnb_s[:], in_=ln_b.rearrange("(a b) -> b a", b=128),
            allow_slow_non_contiguous=True), writes=["lnb_s"])
        ldw(arow[:], a_log.rearrange("(a b) -> a b", a=1), "arow")
        ldw(rbias_s[0:1, 0:1], rbias, "rbias_r")
        rec.dma(GP, "dG", lambda e: e.dma_start(out=ident_b[:], in_=eye_in),
                writes=["ident_b"])
        rec.dma(GP, "dG", lambda e: e.dma_start(out=sel16[:], in_=sel16_in),
                writes=["sel16"])
        rec.dma(GP, "dG", lambda e: e.dma_start(
            out=rw_s[:], in_=rw.rearrange("(t p) e -> p t e", p=128)),
            writes=["rw_s"])
        for (src, dst, key) in [
            (wlT, wlT_b, "wlT_b"), (convd, convd_b, "convd_b"),
            (wbc, wbc_b, "wbc_b"), (wdT, wdT_b, "wdT_b"),
            (wrT, wrT_b, "wrT_b"), (woT, woT_b, "woT_b"),
            (sgw, sgw_b, "sgw_b"), (suw, suw_b, "suw_b"), (sdw, sdw_b, "sdw_b"),
            (gw, gw_b, "gw_b"), (uw, uw_b, "uw_b"), (dw, dw_b, "dw_b"),
        ]:
            rec.dma(GP, "dG", lambda e, s=src, d=dst: e.dma_start(out=d, in_=s),
                    writes=[key])

        rec.op(ACT, lambda e: e.activation(arow2[:], arow[:], AF.Exp),
               reads=["arow"], writes=["arow2"])
        rec.op(ACT, lambda e: e.activation(arow2[:], arow2[:], AF.Ln, bias=1.0),
               reads=["arow2"], writes=["arow2"])
        rec.op(DVE, lambda e: e.tensor_scalar(arow2[:], arow2[:], -1.0, None,
                                              ALU.mult),
               reads=["arow2"], writes=["arow2"])
        rec.op(PE, lambda e: e.matmul(pR[:, 0:NST], ones1[:], arow2[:],
                                      start=True, stop=True),
               reads=["arow2", "ones1"], writes=["pR"])
        rec.op(ACT, lambda e: e.copy(a_bc[:], pR[:, 0:NST]),
               reads=["pR"], writes=["a_bc"])
        rec.op(PE, lambda e: e.matmul(pR[:, 0:1], ones1[:], rbias_s[0:1, 0:1],
                                      start=True, stop=True),
               reads=["rbias_r", "ones1"], writes=["pR"])
        rec.op(ACT, lambda e: e.copy(rbias_s[:], pR[:, 0:1]),
               reads=["pR"], writes=["rbias_s"])

        def layer_norm(src_ap, src_keys, dst_ap, dst_keys, src_is_psum=False):
            hack = os.environ.get("K_HACK", "")
            for h in range(2):
                if hack == "copy":
                    rec.op(DVE, lambda e, h=h: e.tensor_copy(
                        stats[:, 6 * h:6 * h + 6],
                        src_ap[:, 512 * h:512 * h + 6 + 512 * 0][:, 0:6]),
                        reads=src_keys, writes=["stats"])
                    continue
                rec.op(DVE, lambda e, h=h: e.bn_stats(
                    stats[:, 6 * h:6 * h + 6],
                    src_ap[:, 512 * h:512 * h + 512]),
                    reads=src_keys, writes=["stats"], slow=True)
            rec.op(DVE, lambda e: e.bn_aggr(mvv[:], stats[:]),
                   reads=["stats"], writes=["mvv"], slow=True)
            rec.op(DVE, lambda e: e.tensor_scalar(
                c_sq[:], mvv[:, 1:2], 1e-5, None, ALU.add),
                reads=["mvv"], writes=["c_sq"])
            rec.op(ACT, lambda e: e.activation(c_sq[:], c_sq[:], AF.Sqrt),
                   reads=["c_sq"], writes=["c_sq"])
            rec.op(DVE, lambda e: e.reciprocal(c_r[:], c_sq[:]),
                   reads=["c_sq"], writes=["c_r"])
            rec.op(DVE, lambda e: e.tensor_scalar(
                c_mr[:], mvv[:, 0:1], c_r[:, 0:1], -1.0, ALU.mult, ALU.mult),
                reads=["mvv"], sreads=["c_r"], writes=["c_mr"])
            rec.op(ACT, lambda e: e.activation(
                dst_ap, src_ap, AF.Identity, bias=c_mr[:, 0:1],
                scale=c_r[:, 0:1]),
                reads=list(src_keys) + ["c_mr", "c_r"], writes=dst_keys)

        for b in range(B_CORE):
            rec.op(DVE, lambda e: e.memset(hstate[:], 0.0), writes=["hstate"])
            rec.op(DVE, lambda e: e.memset(tails[:], 0.0), writes=["tails"])
            rec.dma(SP, "dW", lambda e: e.dma_start(
                out=wbc_s[:], in_=wbc_b.rearrange("(k p) c -> p k c", p=128)),
                reads=["wbc_b"], writes=["wbc_s"])

            for tc in range(NTC):
                t0 = tc * T_C
                rec.dma(ACT, "dA", lambda e, b=b, t0=t0: e.dma_start(
                    out=x_sl[:], in_=x_in[b, t0:t0 + T_C, :]
                    .rearrange("(i p) d -> p i d", p=128)),
                    writes=[("xr", i2) for i2 in range(TT)] +
                           [("xl32", d2) for d2 in range(8, DIT)])
                for i in range(TT):
                    layer_norm(x_sl[:, i, :], [("xr", i)],
                               xn1_tm[:, i, :], [("xn1t", i)])
                for i in range(TT):
                    for j in range(DMT):
                        rec.op(PE, lambda e, i=i, j=j: e.transpose(
                            pX[:], xn1_tm[:, i, 128 * j:128 * j + 128],
                            ident_b[:]),
                            reads=[("xn1t", i), "ident_b"], writes=["pX"])
                        rec.op(ACT, lambda e, i=i, j=j: e.copy(
                            xn1_fm[:, j, 128 * i:128 * i + 128], pX[:]),
                            reads=["pX"], writes=[("xn1f", j)])

                for dit in range(DIT):
                    pr = dit % 2
                    psL = pA if pr == 0 else pB
                    psC = yA if pr == 0 else yB
                    kL = "pA" if pr == 0 else "pB"
                    kC = "yA" if pr == 0 else "yB"
                    rec.dma(SP, "dW", lambda e, dit=dit, pr=pr: e.dma_start(
                        out=wl_st[pr][:],
                        in_=wlT_b.rearrange("(q p) c -> p q c", p=128)
                        [:, :, 128 * dit:128 * dit + 128]),
                        reads=["wlT_b"], writes=[("wl", pr)])
                    rec.dma(SP, "dW", lambda e, dit=dit, pr=pr: e.dma_start(
                        out=dg_st[pr][:],
                        in_=convd_b[dit].rearrange("k p c -> p k c")),
                        reads=["convd_b"], writes=[("dg", pr)])
                    for j in range(DMT):
                        rec.op(PE, lambda e, j=j, pr=pr, psL=psL: e.matmul(
                            psL[:], wl_st[pr][:, j, :], xn1_fm[:, j, :],
                            start=(j == 0), stop=(j == DMT - 1)),
                            reads=[("wl", pr), ("xn1f", j)], writes=[kL])
                    rec.op(ACT, lambda e, pr=pr, psL=psL: e.copy(
                        leftb[pr][:, 3:3 + T_C], psL[:]),
                        reads=[kL], writes=[("lb", pr)])
                    rec.op(ACT, lambda e, pr=pr, dit=dit: e.copy(
                        leftb[pr][:, 0:3], tails[:, dit, :]),
                        reads=["tails"], writes=[("lb", pr)])
                    rec.op(ACT, lambda e, pr=pr, dit=dit: e.copy(
                        tails[:, dit, :], leftb[pr][:, T_C:T_C + 3]),
                        reads=[("lb", pr)], writes=["tails"])
                    for k in range(4):
                        rec.op(PE, lambda e, k=k, pr=pr, psC=psC: e.matmul(
                            psC[:], dg_st[pr][:, k, :], leftb[pr][:, k:k + T_C],
                            start=(k == 0), stop=(k == 3)),
                            reads=[("dg", pr), ("lb", pr)], writes=[kC])
                    rec.op(ACT, lambda e, dit=dit, psC=psC: e.activation(
                        xl_bf[:, dit, :], psC[:], AF.Silu,
                        bias=convb_s[:, dit:dit + 1]),
                        reads=[kC, "convb_s"], writes=[("xlb", dit)])
                    rec.op(ACT, lambda e, dit=dit, psC=psC: e.activation(
                        xl32[:, dit, :], psC[:], AF.Silu,
                        bias=convb_s[:, dit:dit + 1]),
                        reads=[kC, "convb_s"], writes=[("xl32", dit)])

                for kt in range(DIT):
                    rec.op(PE, lambda e, kt=kt: e.matmul(
                        pA[0:2 * NST, :], wbc_s[:, kt, :], xl_bf[:, kt, :],
                        start=(kt == 0), stop=(kt == DIT - 1)),
                        reads=["wbc_s", ("xlb", kt)], writes=["pA"])
                rec.op(ACT, lambda e: e.copy(bc_bf[:], pA[0:2 * NST, :]),
                       reads=["pA"], writes=["bc_bf"])
                rec.dma(SP, "dA", lambda e: e.dma_start(
                    out=cc_bf[:], in_=bc_bf[NST:2 * NST, :]),
                    reads=["bc_bf"], writes=["cc_bf"])
                for n in range(NST):
                    pr = n % 2
                    ps = yA if pr == 0 else yB
                    kk = "yA" if pr == 0 else "yB"
                    rec.op(PE, lambda e, n=n, ps=ps: e.matmul(
                        ps[:], sel16[:, 128 * n:128 * n + 128],
                        bc_bf[0:NST, :], start=True, stop=True),
                        reads=["bc_bf", "sel16"], writes=[kk])
                    rec.op(ACT, lambda e, n=n, ps=ps: e.copy(Bb[:, n, :], ps[:]),
                           reads=[kk], writes=[("Bb", n)])
                    rec.op(PE, lambda e, n=n, ps=ps: e.matmul(
                        ps[:], sel16[:, 128 * n:128 * n + 128],
                        cc_bf[:], start=True, stop=True),
                        reads=["cc_bf", "sel16"], writes=[kk])
                    rec.op(ACT, lambda e, n=n, ps=ps: e.copy(Cb[:, n, :], ps[:]),
                           reads=[kk], writes=[("Cb", n)])

                for dit in range(DIT):
                    pr = dit % 2
                    psD = pA if pr == 0 else pB
                    kD = "pA" if pr == 0 else "pB"
                    psY = yA if pr == 0 else yB
                    kY = "yA" if pr == 0 else "yB"
                    rec.dma(SP, "dW", lambda e, dit=dit, pr=pr: e.dma_start(
                        out=wd_st[pr][:],
                        in_=wdT_b.rearrange("(q p) c -> p q c", p=128)
                        [:, :, 128 * dit:128 * dit + 128]),
                        reads=["wdT_b"], writes=[("wd", pr)])
                    for kt in range(DIT):
                        rec.op(PE, lambda e, kt=kt, pr=pr, psD=psD: e.matmul(
                            psD[:], wd_st[pr][:, kt, :], xl_bf[:, kt, :],
                            start=(kt == 0), stop=(kt == DIT - 1)),
                            reads=[("wd", pr), ("xlb", kt)], writes=[kD])
                    rec.op(ACT, lambda e, dit=dit, pr=pr, psD=psD: e.activation(
                        dl[pr][:], psD[:], AF.Exp,
                        bias=bdel_s[:, dit:dit + 1]),
                        reads=[kD, "bdel_s"], writes=[("dl", pr)])
                    rec.op(ACT, lambda e, pr=pr: e.activation(
                        dl[pr][:], dl[pr][:], AF.Ln, bias=1.0),
                        reads=[("dl", pr)], writes=[("dl", pr)])
                    rec.op(DVE, lambda e, pr=pr: e.tensor_scalar(
                        dl[pr][:], dl[pr][:], 1e-4, 10.0, ALU.max, ALU.min),
                        reads=[("dl", pr)], writes=[("dl", pr)])
                    rec.op(DVE, lambda e, dit=dit, pr=pr: e.tensor_tensor(
                        dxl[pr][:], dl[pr][:], xl32[:, dit, :], ALU.mult),
                        reads=[("dl", pr), ("xl32", dit)], writes=[("dxl", pr)])
                    for n in range(NST):
                        sp = n % 2
                        rec.op(ACT, lambda e, n=n, pr=pr, sp=sp: e.activation(
                            Gl[sp][:], dl[pr][:], AF.Exp, scale=a_bc[:, n:n + 1]),
                            reads=[("dl", pr), "a_bc"], writes=[("Gl", sp)])
                        rec.op(GP, lambda e, n=n, pr=pr, sp=sp: e.tensor_tensor(
                            ul[sp][:], dxl[pr][:], Bb[:, n, :], ALU.mult),
                            reads=[("dxl", pr), ("Bb", n)], writes=[("ul", sp)])
                        rec.op(DVE,
                               lambda e, n=n, dit=dit, sp=sp: e.tensor_tensor_scan(
                                   hl[sp][:], Gl[sp][:], ul[sp][:],
                                   hstate[:, dit * NST + n:dit * NST + n + 1],
                                   ALU.mult, ALU.add),
                               reads=[("Gl", sp), ("ul", sp)],
                               sreads=["hstate"], writes=[("hl", sp)],
                               slow=True)
                        rec.op(DVE, lambda e, n=n, dit=dit, sp=sp: e.tensor_copy(
                            hstate[:, dit * NST + n:dit * NST + n + 1],
                            hl[sp][:, T_C - 1:T_C]),
                            reads=[("hl", sp)], writes=["hstate"])
                        rec.op(DVE, lambda e, n=n, sp=sp: e.tensor_tensor(
                            ychl[sp][:], hl[sp][:], Cb[:, n, :], ALU.mult),
                            reads=[("hl", sp), ("Cb", n)], writes=[("ychl", sp)])
                        rec.op(PE, lambda e, n=n, sp=sp, psY=psY: e.matmul(
                            psY[:], ident_b[:], ychl[sp][:],
                            start=(n == 0), stop=(n == NST - 1)),
                            reads=[("ychl", sp), "ident_b"], writes=[kY])
                    rec.dma(SP, "dW", lambda e, dit=dit, pr=pr: e.dma_start(
                        out=wr_st[pr][:],
                        in_=wrT_b.rearrange("(q p) c -> p q c", p=128)
                        [:, :, 128 * dit:128 * dit + 128]),
                        reads=["wrT_b"], writes=[("wr", pr)])
                    for j in range(DMT):
                        rec.op(PE, lambda e, j=j, pr=pr, psD=psD: e.matmul(
                            psD[:], wr_st[pr][:, j, :], xn1_fm[:, j, :],
                            start=(j == 0), stop=(j == DMT - 1)),
                            reads=[("wr", pr), ("xn1f", j)], writes=[kD])
                    rec.op(ACT, lambda e, pr=pr, psD=psD: e.activation(
                        sgl[pr][:], psD[:], AF.Silu),
                        reads=[kD], writes=[("sgl", pr)])
                    rec.op(DVE, lambda e, dit=dit, pr=pr, psY=psY: e.tensor_tensor(
                        z_sl[:, dit, :], psY[:], sgl[pr][:], ALU.mult),
                        reads=[kY, ("sgl", pr)], writes=[("z", dit)])

                for dit in range(DIT):
                    pr = dit % 2
                    rec.op(PE, lambda e, dit=dit: e.matmul(
                        pT[0:1, 0:512], onesc[:], z_sl[:, dit, :],
                        start=(dit == 0), stop=(dit == DIT - 1)),
                        reads=[("z", dit), "onesc"], writes=["pT"])
                    rec.op(ACT, lambda e, dit=dit, pr=pr: e.activation(
                        sgl[pr][:], z_sl[:, dit, :], AF.Square),
                        reads=[("z", dit)], writes=[("sgl", pr)])
                    rec.op(PE, lambda e, dit=dit, pr=pr: e.matmul(
                        pT[0:1, 512:1024], onesc[:], sgl[pr][:],
                        start=(dit == 0), stop=(dit == DIT - 1)),
                        reads=[("sgl", pr), "onesc"], writes=["pT"])
                MU = rr[0:1, 0:512]
                RS = rr[0:1, 512:1024]
                TM = rr[0:1, 1024:1536]
                rec.op(DVE, lambda e: e.tensor_scalar(
                    MU, pT[0:1, 0:512], 1.0 / DI, None, ALU.mult),
                    reads=["pT"], writes=["rr"])
                rec.op(DVE, lambda e: e.tensor_scalar(
                    RS, pT[0:1, 512:1024], 1.0 / DI, None, ALU.mult),
                    reads=["pT"], writes=["rr"])
                rec.op(DVE, lambda e: e.tensor_tensor(
                    TM, MU, MU, ALU.mult), reads=["rr"], writes=["rr"])
                rec.op(DVE, lambda e: e.tensor_tensor(
                    RS, RS, TM, ALU.subtract), reads=["rr"], writes=["rr"])
                rec.op(DVE, lambda e: e.tensor_scalar(
                    RS, RS, 1e-5, None, ALU.add), reads=["rr"], writes=["rr"])
                rec.op(ACT, lambda e: e.activation(RS, RS, AF.Sqrt),
                       reads=["rr"], writes=["rr"])
                rec.op(DVE, lambda e: e.reciprocal(RS, RS),
                       reads=["rr"], writes=["rr"])
                rec.op(DVE, lambda e: e.tensor_tensor(
                    MU, MU, RS, ALU.mult), reads=["rr"], writes=["rr"])
                rec.op(DVE, lambda e: e.tensor_scalar(
                    MU, MU, -1.0, None, ALU.mult), reads=["rr"], writes=["rr"])
                rec.op(PE, lambda e: e.matmul(yA[:], ones1[:], RS,
                                              start=True, stop=True),
                       reads=["rr", "ones1"], writes=["yA"])
                rec.op(PE, lambda e: e.matmul(yB[:], ones1[:], MU,
                                              start=True, stop=True),
                       reads=["rr", "ones1"], writes=["yB"])
                for dit in range(DIT):
                    pr = dit % 2
                    rec.op(DVE, lambda e, dit=dit, pr=pr: e.tensor_tensor(
                        sgl[pr][:], z_sl[:, dit, :], yA[:], ALU.mult),
                        reads=[("z", dit), "yA"], writes=[("sgl", pr)])
                    w_extra = [("Bb", n) for n in range(NST)] + \
                              [("Cb", n) for n in range(NST)] if dit == 0 else []
                    rec.op(DVE, lambda e, dit=dit, pr=pr: e.tensor_tensor(
                        zp_b[:, dit, :], sgl[pr][:], yB[:], ALU.add),
                        reads=[("sgl", pr), "yB"],
                        writes=[("zp", dit)] + w_extra)

                pss = [pA, pB, yA, yB]
                kss = ["pA", "pB", "yA", "yB"]
                for half in range(2):
                    for kt in range(DIT):
                        pr = kt % 2
                        rec.dma(SP, "dW", lambda e, kt=kt, pr=pr: e.dma_start(
                            out=wo_st[pr][:],
                            in_=woT_b[128 * kt:128 * kt + 128, :]),
                            reads=["woT_b"], writes=[("wo", pr)])
                        rec.op(DVE, lambda e, kt=kt, pr=pr: e.tensor_scalar(
                            wo_st[pr][:], wo_st[pr][:], g_s[:, kt:kt + 1], None,
                            ALU.mult),
                            reads=[("wo", pr), "g_s"], writes=[("wo", pr)])
                        if b == 0 and tc == 0 and half == 0:
                            for j in range(DMT):
                                rec.op(PE, lambda e, kt=kt, j=j, pr=pr: e.matmul(
                                    pR[:, j:j + 1],
                                    wo_st[pr][:, 128 * j:128 * j + 128],
                                    lnb_s[:, kt:kt + 1],
                                    start=(kt == 0 and j == 0),
                                    stop=(kt == DIT - 1 and j == DMT - 1),
                                    skip_group_check=True),
                                    reads=[("wo", pr), "lnb_s"], writes=["pR"])
                        for j4 in range(4):
                            j = half * 4 + j4
                            rec.op(PE, lambda e, kt=kt, j=j, pr=pr, j4=j4:
                                   e.matmul(pss[j4][:],
                                            wo_st[pr][:, 128 * j:128 * j + 128],
                                            zp_b[:, kt, :], start=(kt == 0),
                                            stop=(kt == DIT - 1),
                                            skip_group_check=True),
                                   reads=[("wo", pr), ("zp", kt)],
                                   writes=[kss[j4]])
                    if b == 0 and tc == 0 and half == 0:
                        rec.op(ACT, lambda e: e.copy(bvec[:], pR[:, 0:DMT]),
                               reads=["pR"], writes=["bvec"])
                    for j4 in range(4):
                        j = half * 4 + j4
                        wz = [("mo", j), ("z", 2 * j), ("z", 2 * j + 1)]
                        rec.op(ACT, lambda e, j=j, j4=j4: e.activation(
                            mo_fm[:, j, :], pss[j4][:], AF.Identity,
                            bias=bvec[:, j:j + 1]),
                            reads=[kss[j4], "bvec"], writes=wz)

                rec.dma(ACT, "dA", lambda e, b=b, t0=t0: e.dma_start(
                    out=x_sl[:], in_=x_in[b, t0:t0 + T_C, :]
                    .rearrange("(i p) d -> p i d", p=128)),
                    writes=[("xr", i2) for i2 in range(TT)] +
                           [("xl32", d2) for d2 in range(8, DIT)])
                for i in range(TT):
                    for j in range(DMT):
                        rec.op(PE, lambda e, i=i, j=j: e.transpose(
                            pT[:, 128 * j:128 * j + 128],
                            mo_fm[:, j, 128 * i:128 * i + 128], ident_f[:]),
                            reads=[("mo", j), "ident_f"], writes=["pT"])
                    layer_norm(pT[:, 0:DM], ["pT"], out_tile[:],
                               ["out_tile", ("hl", 0), ("hl", 1)])
                    rec.op(DVE, lambda e, i=i: e.tensor_tensor(
                        h2_sl[:, i, :], out_tile[:], x_sl[:, i, :], ALU.add),
                        reads=["out_tile", ("xr", i)],
                        writes=[("h2w", i), ("xl32", i * 2), ("xl32", i * 2 + 1)])
                rec.dma(SP, "dS", lambda e, b=b, t0=t0: e.dma_start(
                    out=h2_sp[b, t0:t0 + T_C, :]
                    .rearrange("(i p) d -> p i d", p=128), in_=h2_sl[:]),
                    reads=[("h2w", i2) for i2 in range(TT)] +
                          [("xl32", d2) for d2 in range(8)],
                    writes=[("h2hbm", b, t0 // T_C)])

            rec.fence()
            for tc in range(NTC):
                t0 = tc * T_C
                rec.dma(SP, "dS", lambda e, b=b, t0=t0, tc=tc: e.dma_start(
                    out=h2r[:], in_=h2_sp[b, t0:t0 + T_C, :]
                    .rearrange("(i p) d -> p i d", p=128)),
                    reads=[("h2hbm", b, tc)],
                    writes=[("h2r", i2) for i2 in range(TT)])
                for i in range(TT):
                    layer_norm(h2r[:, i, :], [("h2r", i)],
                               xn1_tm[:, i, :], [("xn1t", i)])
                    for j in range(DMT):
                        rec.op(PE, lambda e, i=i, j=j: e.transpose(
                            pX[:], xn1_tm[:, i, 128 * j:128 * j + 128],
                            ident_b[:]),
                            reads=[("xn1t", i), "ident_b"], writes=["pX"])
                        rec.op(ACT, lambda e, i=i, j=j, t0=t0: e.copy(
                            xn2_fm[:, j, t0 + 128 * i:t0 + 128 * i + 128],
                            pX[:]),
                            reads=["pX"], writes=[("xn2", tc, j)])
                for i in range(TT):
                    for j in range(DMT):
                        rec.op(PE, lambda e, i=i, j=j, t0=t0: e.matmul(
                            pR[:, 0:EXPERTS],
                            xn2_fm[:, j, t0 + 128 * i:t0 + 128 * i + 128],
                            rw_s[:, j, :], start=(j == 0), stop=(j == DMT - 1)),
                            reads=[("xn2", tc, j), "rw_s"], writes=["pR"])
                    rec.op(ACT, lambda e, i=i: e.copy(
                        scoret[:, i, :], pR[:, 0:EXPERTS]),
                        reads=["pR"], writes=["scoret"])
                    if b == 0 and tc == 0 and i == 0:
                        rec.op(DVE, lambda e: e.tensor_copy(
                            dbg_sb[:, 0:8], scoret[:, 0, :]),
                            reads=["scoret"], writes=["dbg_sb"])
                    rec.op(DVE, lambda e, i=i: e.reduce_max(
                        rmax[:], scoret[:, i, :], axis=AX.X, negate=True),
                        reads=["scoret"], writes=["rmax"], slow=True)
                    rec.op(ACT, lambda e, i=i: e.activation(
                        wdnt[:, i, :], scoret[:, i, :], AF.Exp,
                        bias=rmax[:, 0:1], accum_out=rsum[:]),
                        reads=["scoret", "rmax"], writes=["wdnt", "rsum"])
                    rec.op(DVE, lambda e: e.reciprocal(rmax[:], rsum[:]),
                           reads=["rsum"], writes=["rmax"], slow=True)
                    rec.op(ACT, lambda e, i=i: e.activation(
                        scoret[:, i, :], wdnt[:, i, :], AF.Identity,
                        scale=rmax[:, 0:1]),
                        reads=["wdnt", "rmax"], writes=["scoret"])
                    if b == 0 and tc == 0 and i == 0:
                        rec.op(DVE, lambda e: e.tensor_copy(
                            dbg_sb[:, 8:16], wdnt[:, 0, :]),
                            reads=["wdnt"], writes=["dbg_sb"])
                        rec.op(DVE, lambda e: e.tensor_copy(
                            dbg_sb[:, 16:24], scoret[:, 0, :]),
                            reads=["scoret"], writes=["dbg_sb"])
                    rec.op(DVE, lambda e, i=i: e.tensor_scalar(
                        wdnt[:, i, :], scoret[:, i, :], rbias_s[:, 0:1], None,
                        ALU.add),
                        reads=["scoret"], sreads=["rbias_s"], writes=["wdnt"])
                    rec.op(DVE, lambda e, i=i: e.max(top8[:], wdnt[:, i, :]),
                           reads=[], sreads=["wdnt"], writes=["top8"],
                           slow=True)
                    rec.op(DVE, lambda e, i=i: e.tensor_single_scalar(
                        maskt[:], wdnt[:, i, :], top8[:, 3:4], ALU.is_ge),
                        reads=[], sreads=["top8", "wdnt"], writes=["maskt"])
                    if b == 0 and tc == 0 and i == 0:
                        rec.op(DVE, lambda e: e.tensor_copy(
                            dbg_sb[:, 24:32], top8[:, :]),
                            reads=["top8"], writes=["dbg_sb"])
                        rec.op(DVE, lambda e: e.tensor_copy(
                            dbg_sb[:, 32:40], maskt[:, :]),
                            reads=["maskt"], writes=["dbg_sb"])
                    rec.op(DVE, lambda e, i=i: e.tensor_tensor(
                        wdnt[:, i, :], scoret[:, i, :], maskt[:], ALU.mult),
                        reads=[], sreads=["scoret", "maskt"], writes=["wdnt"])
                    rec.op(DVE, lambda e, i=i: e.reduce_sum(
                        wsum[:], wdnt[:, i, :], axis=AX.X),
                        reads=[], sreads=["wdnt"], writes=["wsum"], slow=True)
                    rec.op(DVE, lambda e: e.tensor_scalar(
                        wsum[:], wsum[:], 1e-20, None, ALU.add),
                        reads=["wsum"], writes=["wsum"])
                    rec.op(DVE, lambda e: e.reciprocal(rsum[:], wsum[:]),
                           reads=["wsum"], writes=["rsum"], slow=True)
                    rec.op(ACT, lambda e, i=i: e.activation(
                        wdnt[:, i, :], wdnt[:, i, :], AF.Identity,
                        scale=rsum[:, 0:1]),
                        reads=["wdnt", "rsum"], writes=["wdnt"])
                    rec.op(DVE, lambda e, i=i, tc=tc: e.tensor_copy(
                        wdn_all[:, tc * TT + i, :], wdnt[:, i, :]),
                        reads=[], sreads=["wdnt"], writes=["wdn_all"])

            if b == 0:
                rec.dma(SP, "dS", lambda e: e.dma_start(
                    out=dbg_wdn, in_=wdn_all[:]),
                    reads=["wdn_all"], writes=[("DBGW",)])
                rec.dma(SP, "dS", lambda e: e.dma_start(
                    out=dbg_wbc, in_=dbg_sb[:]),
                    reads=["dbg_sb"], writes=[("DBGB",)])
            rec.fence()
            for ig in range(4):
                rec.dma(SP, "dW", lambda e, ig=ig: e.dma_start(
                    out=sgst[:], in_=sgw_b.rearrange(
                        "(q p) (r c) -> p q r c", p=128, c=128)
                    [:, :, 4 * ig:4 * ig + 4, :]),
                    reads=["sgw_b"], writes=["sgst"])
                rec.dma(SP, "dW", lambda e, ig=ig: e.dma_start(
                    out=sust[:], in_=suw_b.rearrange(
                        "(q p) (r c) -> p q r c", p=128, c=128)
                    [:, :, 4 * ig:4 * ig + 4, :]),
                    reads=["suw_b"], writes=["sust"])
                rec.dma(SP, "dW", lambda e, ig=ig: e.dma_start(
                    out=sd_st[:], in_=sdw_b[512 * ig:512 * ig + 512, :]
                    .rearrange("(q p) c -> p q c", p=128)),
                    reads=["sdw_b"], writes=["sd"])
                for tc in range(NTC):
                    t0 = tc * T_C
                    for ii in range(4):
                        pp = ii % 2
                        psg = pA if pp == 0 else pB
                        ksg = "pA" if pp == 0 else "pB"
                        psu = yA if pp == 0 else yB
                        ksu = "yA" if pp == 0 else "yB"
                        for j in range(DMT):
                            rec.op(PE, lambda e, ii=ii, j=j, psg=psg, t0=t0:
                                   e.matmul(psg[:], sgst[:, j, ii, :],
                                            xn2_fm[:, j, t0:t0 + T_C],
                                            start=(j == 0), stop=(j == DMT - 1)),
                                   reads=["sgst", ("xn2", tc, j)], writes=[ksg])
                            rec.op(PE, lambda e, ii=ii, j=j, psu=psu, t0=t0:
                                   e.matmul(psu[:], sust[:, j, ii, :],
                                            xn2_fm[:, j, t0:t0 + T_C],
                                            start=(j == 0), stop=(j == DMT - 1)),
                                   reads=["sust", ("xn2", tc, j)], writes=[ksu])
                        rec.op(ACT, lambda e, psg=psg, pp=pp: e.activation(
                            sgl[pp][:], psg[:], AF.Silu),
                            reads=[ksg], writes=[("sgl", pp)])
                        rec.op(DVE, lambda e, ii=ii, psu=psu, pp=pp:
                               e.tensor_tensor(saug[:, ii, :], sgl[pp][:],
                                               psu[:], ALU.mult),
                               reads=[("sgl", pp), ksu], writes=[("saug", ii)])
                    for j in range(DMT):
                        jp = j % 2
                        for ii in range(4):
                            rec.op(PE, lambda e, ii=ii, j=j, jp=jp: e.matmul(
                                pT[:, 512 * jp:512 * jp + 512],
                                sd_st[:, ii, 128 * j:128 * j + 128],
                                saug[:, ii, :], start=(ii == 0), stop=(ii == 3)),
                                reads=["sd", ("saug", ii)], writes=[("pTd", jp)])
                        if ig == 0:
                            rec.op(ACT, lambda e, j=j, jp=jp, t0=t0: e.copy(
                                routed[:, j, t0:t0 + T_C],
                                pT[:, 512 * jp:512 * jp + 512]),
                                reads=[("pTd", jp)], writes=[("routed", tc, j)])
                        else:
                            rec.op(DVE, lambda e, j=j, jp=jp, t0=t0:
                                   e.tensor_tensor(
                                       routed[:, j, t0:t0 + T_C],
                                       routed[:, j, t0:t0 + T_C],
                                       pT[:, 512 * jp:512 * jp + 512], ALU.add),
                                   reads=[("routed", tc, j), ("pTd", jp)],
                                   writes=[("routed", tc, j)])

            rec.fence()
            for ex in range(EXPERTS):
                rec.dma(SP, "dW", lambda e, ex=ex: e.dma_start(
                    out=wg_st[:],
                    in_=gw_b[ex].rearrange("(q p) c -> p q c", p=128)),
                    reads=["gw_b"], writes=["wg"])
                rec.dma(SP, "dW", lambda e, ex=ex: e.dma_start(
                    out=wu_st[:],
                    in_=uw_b[ex].rearrange("(q p) c -> p q c", p=128)),
                    reads=["uw_b"], writes=["wu"])
                rec.dma(SP, "dW", lambda e, ex=ex: e.dma_start(
                    out=wdown_st[:],
                    in_=dw_b[ex].rearrange("(q p) c -> p q c", p=128)),
                    reads=["dw_b"], writes=["wdown"])
                for tc in range(NTC):
                    t0 = tc * T_C
                    for i in range(TT):
                        rec.op(PE, lambda e, ex=ex, tc=tc, i=i: e.matmul(
                            pR[:, 128 * i:128 * i + 128],
                            wdn_all[:, tc * TT + i, ex:ex + 1]
                            .broadcast_to([128, 128]),
                            ident_f[:], start=True, stop=True),
                            reads=["wdn_all", "ident_f"], writes=["pR"])
                    rec.op(ACT, lambda e: e.copy(wbcast[:], pR[:]),
                           reads=["pR"], writes=["wbcast"])
                    for it in range(4):
                        ip = it % 2
                        psg = pA if ip == 0 else pB
                        ksg = "pA" if ip == 0 else "pB"
                        psu = yA if ip == 0 else yB
                        ksu = "yA" if ip == 0 else "yB"
                        for j in range(DMT):
                            rec.op(PE, lambda e, it=it, j=j, psg=psg, t0=t0:
                                   e.matmul(psg[:],
                                            wg_st[:, j, 128 * it:128 * it + 128],
                                            xn2_fm[:, j, t0:t0 + T_C],
                                            start=(j == 0), stop=(j == DMT - 1)),
                                   reads=["wg", ("xn2", tc, j)], writes=[ksg])
                            rec.op(PE, lambda e, it=it, j=j, psu=psu, t0=t0:
                                   e.matmul(psu[:],
                                            wu_st[:, j, 128 * it:128 * it + 128],
                                            xn2_fm[:, j, t0:t0 + T_C],
                                            start=(j == 0), stop=(j == DMT - 1)),
                                   reads=["wu", ("xn2", tc, j)], writes=[ksu])
                        rec.op(ACT, lambda e, psg=psg, ip=ip: e.activation(
                            sgl[ip][:], psg[:], AF.Silu),
                            reads=[ksg], writes=[("sgl", ip)])
                        rec.op(DVE, lambda e, psu=psu, ip=ip: e.tensor_tensor(
                            hl[ip][:], sgl[ip][:], psu[:], ALU.mult),
                            reads=[("sgl", ip), ksu], writes=[("hl", ip)])
                        rec.op(DVE, lambda e, it=it, ip=ip: e.tensor_tensor(
                            ap_sl[it][:], hl[ip][:], wbcast[:], ALU.mult),
                            reads=[("hl", ip), "wbcast"], writes=[("ap", it)])
                    for j in range(DMT):
                        jp = j % 2
                        for it in range(4):
                            rec.op(PE, lambda e, it=it, j=j, jp=jp: e.matmul(
                                pT[:, 512 * jp:512 * jp + 512],
                                wdown_st[:, it, 128 * j:128 * j + 128],
                                ap_sl[it][:], start=(it == 0), stop=(it == 3)),
                                reads=["wdown", ("ap", it)], writes=[("pTd", jp)])
                        rec.op(DVE, lambda e, j=j, jp=jp, t0=t0: e.tensor_tensor(
                            routed[:, j, t0:t0 + T_C], routed[:, j, t0:t0 + T_C],
                            pT[:, 512 * jp:512 * jp + 512], ALU.add),
                            reads=[("routed", tc, j), ("pTd", jp)],
                            writes=[("routed", tc, j)])

            rec.fence()
            for tc in range(NTC):
                t0 = tc * T_C
                rec.dma(ACT, "dA", lambda e, b=b, t0=t0, tc=tc: e.dma_start(
                    out=h2r[:], in_=h2_sp[b, t0:t0 + T_C, :]
                    .rearrange("(i p) d -> p i d", p=128)),
                    reads=[("h2hbm", b, tc)],
                    writes=[("h2r", i2) for i2 in range(TT)])
                for i in range(TT):
                    for j in range(DMT):
                        rec.op(PE, lambda e, i=i, j=j, t0=t0: e.transpose(
                            pT[:, 128 * j:128 * j + 128],
                            routed[:, j, t0 + 128 * i:t0 + 128 * i + 128],
                            ident_f[:]),
                            reads=[("routed", tc, j), "ident_f"], writes=["pT"])
                    layer_norm(pT[:, 0:DM], ["pT"], out_tile[:], ["out_tile"])
                    rec.op(DVE, lambda e, i=i: e.tensor_tensor(
                        out_tile[:], out_tile[:], h2r[:, i, :], ALU.add),
                        reads=["out_tile", ("h2r", i)], writes=["out_tile"])
                    rec.dma(SP, "dS", lambda e, b=b, t0=t0, i=i: e.dma_start(
                        out=out[b, t0 + 128 * i:t0 + 128 * i + 128, :],
                        in_=out_tile[:]),
                        reads=["out_tile"], writes=[("OUT", b, t0, i)])
            rec.fence()

        rec.finalize()

        emitters = {"SP": block.sync, "ACT": block.scalar, "DVE": block.vector,
                    "GP": block.gpsimd, "PE": block.tensor}
        for ename, deco in emitters.items():
            @deco
            def _(eng, ename=ename):
                rec.emit_engine(ename, eng, sems, dma_sems)

    return nc


_NC_CACHE = None


def _host_layout(inputs):
    f = np.ascontiguousarray
    x = f(np.asarray(inputs["L0_attn_x"]))
    conv_w = np.asarray(inputs["L0_attn_conv_w"])
    convd = np.zeros((DIT, 4, 128, 128), np.float32)
    idx = np.arange(128)
    for dit in range(DIT):
        for k in range(4):
            convd[dit, k, idx, idx] = conv_w[dit * 128 + idx, 0, k]
    wbc = np.concatenate([np.asarray(inputs["L0_attn_W_B"]).T,
                          np.asarray(inputs["L0_attn_W_C"]).T], axis=1)
    shared = {
        "wlT": f(np.asarray(inputs["L0_attn_W_left"]).T),
        "wdT": f(np.asarray(inputs["L0_attn_W_delta"]).T),
        "wrT": f(np.asarray(inputs["L0_attn_W_right"]).T),
        "woT": f(np.asarray(inputs["L0_attn_W_out"]).T),
        "wbc": f(wbc),
        "convd": convd,
        "convb": f(np.asarray(inputs["L0_attn_conv_b"])),
        "bdel": f(np.asarray(inputs["L0_attn_b_delta"])),
        "a_log": f(np.asarray(inputs["L0_attn_A_log"])),
        "ln_g": f(np.asarray(inputs["L0_attn_ln_g"])),
        "ln_b": f(np.asarray(inputs["L0_attn_ln_b"])),
        "rw": f(np.asarray(inputs["L0_ffn_router_w"])),
        "rbias": np.asarray(inputs["L0_ffn_router_bias"],
                            np.float32).reshape(1, 1),
        "gw": f(np.asarray(inputs["L0_ffn_gate_w"])),
        "uw": f(np.asarray(inputs["L0_ffn_up_w"])),
        "dw": f(np.asarray(inputs["L0_ffn_down_w"])),
        "sgw": f(np.asarray(inputs["L0_ffn_s_gate_w"])),
        "suw": f(np.asarray(inputs["L0_ffn_s_up_w"])),
        "sdw": f(np.asarray(inputs["L0_ffn_s_down_w"])),
        "eye": np.eye(128, dtype=np.float32),
        "sel16i": np.kron(np.eye(NST, dtype=np.float32),
                          np.ones((1, 128), np.float32)),
        "ones1": np.ones((1, 128), np.float32),
    }
    in_maps = []
    for c in range(N_CORES):
        m = dict(shared)
        m["x"] = f(x[c * B_CORE:(c + 1) * B_CORE])
        in_maps.append(m)
    return in_maps


def kernel(**inputs):
    global _NC_CACHE
    if _NC_CACHE is None:
        _NC_CACHE = build_program()
    nc = _NC_CACHE
    in_maps = _host_layout(inputs)
    res = run_bass_kernel_spmd(nc, in_maps, core_ids=list(range(N_CORES)))
    outs = [np.asarray(res.results[c]["out"]) for c in range(N_CORES)]
    return np.concatenate(outs, axis=0).astype(np.float32)
